# version 04
# baseline (speedup 1.0000x reference)
import numpy as np
import ml_dtypes
from contextlib import ExitStack

import concourse.bass as bass
import concourse.mybir as mybir
from concourse.bass_utils import run_bass_kernel_spmd

F32 = mybir.dt.float32
BF16 = mybir.dt.bfloat16
AF = mybir.ActivationFunctionType
OP = mybir.AluOpType
AX = mybir.AxisListType.X

B, S, CH = 2, 8192, 4096
TOK = 2048
NI = 32
NCHK = 16
SUB = 32
LNB = 256
EPS = 1e-5
BAND = 32


class Emit:
    def __init__(self, nc, stack):
        self.nc = nc
        self.cnt = {}
        self.sem = {}
        self.wm = {}
        for s in ["dS", "dG", "dC", "tS", "vS", "sS", "cS"]:
            self.sem[s] = stack.enter_context(nc.semaphore(s))
            self.cnt[s] = 0
            self.wm[s] = {}

    def inc(self, ins, s, n=1):
        ins.then_inc(self.sem[s], n)
        self.cnt[s] += n
        return (s, self.cnt[s])

    def tok(self, s):
        return (s, self.cnt[s])

    def wait(self, eng, ename, *toks):
        for t in toks:
            if t is None:
                continue
            s, v = t
            if v <= 0 or self.wm[s].get(ename, 0) >= v:
                continue
            eng.wait_ge(self.sem[s], v)
            self.wm[s][ename] = v

    def barrier(self, engs):
        toks = [self.tok(s) for s in self.sem]
        for eng, en in engs:
            self.wait(eng, en, *toks)


def build_nc(dbg=False):
    nc = bass.Bass("TRN2", target_bir_lowering=False)
    d = nc.dram_tensor

    xT = d("xT", [128, NI, TOK + 1], F32, kind="ExternalInput").ap()
    wins = {}
    for nm in ["wk", "wv", "wr", "wg", "wo", "wb"]:
        wins[nm] = d(nm, [NI, 128, CH], F32, kind="ExternalInput").ap()
    wgate = d("wgate", [128, NI, 8], F32, kind="ExternalInput").ap()
    aTin = d("aT", [128, NI, 128], F32, kind="ExternalInput").ap()
    bfin = d("bflat", [128, CH], F32, kind="ExternalInput").ap()
    cols_in = d("cols", [128, NI, 8], F32, kind="ExternalInput").ap()
    bb_in = d("bb", [1, CH], F32, kind="ExternalInput").ap()
    identI = d("identI", [128, 128], BF16, kind="ExternalInput").ap()
    onescI = d("onescI", [128, 128], BF16, kind="ExternalInput").ap()
    indgnI = d("indgnI", [128, 128], BF16, kind="ExternalInput").ap()
    indbdI = d("indbdI", [128, 128], BF16, kind="ExternalInput").ap()
    onesrI = d("onesrI", [1, 512], BF16, kind="ExternalInput").ap()
    EQI = d("EQI", [128, SUB], F32, kind="ExternalInput").ap()
    EKI = d("EKI", [128, SUB], F32, kind="ExternalInput").ap()
    EMI = d("EMI", [128, 64], F32, kind="ExternalInput").ap()
    cexpI = d("cexpI", [128, 4], F32, kind="ExternalInput").ap()
    outF = d("outF", [128, NI, TOK], F32, kind="ExternalOutput").ap()

    kw = dict(kind="ExternalOutput") if dbg else {}
    xnS = d("xnS", [128, NI, TOK], BF16, **kw)
    mS = [d(f"m{i}S", [128, NI, TOK], BF16, **kw) for i in range(4)]
    kS = d("kS", [128, NCHK, NI, 128], BF16, **kw)
    vS = d("vS", [128, NCHK, NI, 128], BF16, **kw)
    rS = d("rS", [128, NCHK, NI, 128], BF16, **kw)
    gS = d("gS", [128, NCHK, NI, 128], BF16, **kw)
    OSs = d("OSs", [128, NCHK, NI, 128], F32, **kw)
    gngS = d("gngS", [128, NCHK, NI, 128], BF16, **kw)
    outP = d("outP", [128, NI, TOK], F32, **kw)
    stS = d("stS", [128, 64, TOK], BF16, **kw)
    ccin = d("ccin", [128, TOK], BF16)
    ccout = d("ccout", [4, 128, TOK], BF16)
    ccdbg = d("ccdbg", [4, 128, TOK], BF16, **kw) if dbg else None
    kdbg = d("kdbg", [128, NCHK, NI, 128], BF16, **kw) if dbg else None
    hwdbg = d("hwdbg", [128, TOK], BF16, **kw) if dbg else None
    rtdbg = d("rtdbg", [128, NCHK, 8], F32, **kw) if dbg else None

    st = ExitStack()
    with st:
        E = Emit(nc, st)
        sb = lambda name, shape, dt: st.enter_context(
            nc.sbuf_tensor(name, shape, dt))

        cols = sb("colsb", [128, NI, 8], F32)
        lw = sb("lw", [128, NI], F32)
        wcol = sb("wcol", [128, NI], F32)
        wt32 = sb("wt32", [128, NI], F32)
        wpow = sb("wpow", [128, NI, 64], F32)
        cinc = sb("cinc", [128, NI, 4], F32)
        Dq = sb("Dq", [128, NI, SUB], BF16)
        Dkf = sb("Dkf", [128, NI, SUB], BF16)
        ident = sb("ident", [128, 128], BF16)
        onesc = sb("onesc", [128, 128], BF16)
        indgn = sb("indgn", [128, 128], BF16)
        indbd = sb("indbd", [128, 128], BF16)
        onesr = sb("onesr", [1, 512], BF16)
        wgate_s = sb("wgate_s", [128, NI, 8], BF16)
        aT_s = sb("aT_s", [128, NI, 128], BF16)
        bflat_s = sb("bflat_s", [128, CH], BF16)
        bb_s = sb("bb_s", [1, CH], BF16)
        route = sb("route", [128, NCHK, 8], F32)
        hwfm = sb("hwfm", [128, TOK], BF16)
        Sst = sb("Sst", [128, NI, 64], F32)
        Sin = sb("Sin", [128, NI, 64], F32)
        eq = sb("eqb", [128, SUB], F32)
        ek = sb("ekb", [128, SUB], F32)
        em = sb("emb", [128, 64], F32)
        cexp = sb("cexpb", [128, 4], F32)

        sy, gp, sc, ve, te = nc.sync, nc.gpsimd, nc.scalar, nc.vector, nc.tensor
        ENGS = [(sy, "sy"), (gp, "gp"), (sc, "sc"), (ve, "ve"), (te, "te")]

        def vTT(out, a, b, op):
            return E.inc(ve.tensor_tensor(out, a, b, op), "vS")

        def vTS(out, a, s1, op):
            return E.inc(ve.tensor_scalar(out, a, s1, None, op), "vS")

        def vCP(out, a):
            return E.inc(ve.tensor_copy(out, a), "vS")

        for dst, src in [(cols, cols_in), (ident, identI), (onesc, onescI),
                         (indgn, indgnI), (indbd, indbdI), (onesr, onesrI),
                         (eq, EQI), (ek, EKI), (em, EMI), (cexp, cexpI)]:
            E.inc(sy.dma_start(out=dst[:], in_=src), "dS", 16)
        ld0 = E.tok("dS")
        for dst, src in [(wgate_s, wgate), (aT_s, aTin), (bflat_s, bfin),
                         (bb_s, bb_in)]:
            E.inc(gp.dma_start(out=dst[:], in_=src), "dG", 16)
        ldg = E.tok("dG")

        E.wait(sc, "sc", ld0)
        t1 = E.inc(sc.activation(lw[:], cols[:, :, 0], AF.Exp), "sS")
        E.wait(ve, "ve", t1)
        t1 = vTS(lw[:], lw[:], -1.0, OP.mult)
        E.wait(sc, "sc", t1)
        E.inc(sc.activation(wcol[:], lw[:], AF.Exp), "sS")
        E.inc(sc.activation(wt32[:], lw[:], AF.Exp, scale=32.0), "sS")
        lwb = lambda n: lw[:, :, None].broadcast_to((128, NI, n))
        E.wait(ve, "ve", t1)
        t2 = vTT(wpow[:], em[:, None, :].broadcast_to((128, NI, 64)), lwb(64),
                 OP.mult)
        E.wait(sc, "sc", t2)
        E.inc(sc.activation(wpow[:], wpow[:], AF.Exp), "sS")
        t2 = vTT(cinc[:], cexp[:, None, :].broadcast_to((128, NI, 4)), lwb(4),
                 OP.mult)
        E.wait(sc, "sc", t2)
        E.inc(sc.activation(cinc[:], cinc[:], AF.Exp), "sS")
        t2 = vTT(Dq[:], eq[:, None, :].broadcast_to((128, NI, SUB)), lwb(SUB),
                 OP.mult)
        E.wait(sc, "sc", t2)
        E.inc(sc.activation(Dq[:], Dq[:], AF.Exp), "sS")
        t2 = vTT(Dkf[:], ek[:, None, :].broadcast_to((128, NI, SUB)), lwb(SUB),
                 OP.mult)
        E.wait(sc, "sc", t2)
        E.inc(sc.activation(Dkf[:], Dkf[:], AF.Exp), "sS")
        E.inc(ve.memset(Sst[:], 0.0), "vS")
        E.wait(ve, "ve", ldg)
        E.barrier(ENGS)

        with (
            nc.sbuf_tensor("xblk", [128, NI, LNB + 1], BF16) as xblk,
            nc.sbuf_tensor("xnblk", [128, NI, LNB + 1], BF16) as xnblk,
            nc.sbuf_tensor("xsq", [128, 2, LNB + 1], BF16) as xsq,
            nc.sbuf_tensor("dlt", [128, LNB], BF16) as dlt,
            nc.sbuf_tensor("mixb", [128, 2, LNB], BF16) as mixb,
            nc.sbuf_tensor("lnt", [128, 4, LNB + 1], F32) as lnt,
            nc.psum_tensor("lnp", [128, 2, 512], F32) as lnp,
        ):
            mspill = [None, None]
            for b in range(8):
                c0 = b * LNB
                E.wait(gp, "gp", E.tok("vS"), E.tok("sS"))
                ld = E.inc(gp.dma_start(out=xblk[:],
                                        in_=xT[:, :, c0:c0 + LNB + 1]), "dG", 16)
                E.wait(te, "te", E.tok("vS"))
                for a in range(NI):
                    E.wait(sc, "sc", ld, E.tok("tS") if a >= 2 else None)
                    sq = E.inc(sc.activation(xsq[:, a % 2, :], xblk[:, a, :],
                                             AF.Square), "sS")
                    E.wait(te, "te", ld, sq)
                    E.inc(te.matmul(lnp[:, 0, 0:LNB + 1], onesc[:], xblk[:, a, :],
                                    start=(a == 0), stop=(a == NI - 1)), "tS")
                    mmt = E.inc(te.matmul(lnp[:, 1, 0:LNB + 1], onesc[:], xsq[:, a % 2, :],
                                          start=(a == 0), stop=(a == NI - 1)),
                                "tS")
                E.wait(ve, "ve", mmt)
                vCP(lnt[:, 0, :], lnp[:, 0, 0:LNB + 1])
                vCP(lnt[:, 1, :], lnp[:, 1, 0:LNB + 1])
                vTT(lnt[:, 2, :], lnt[:, 0, :], lnt[:, 0, :], OP.mult)
                vTT(lnt[:, 1, :], lnt[:, 1, :], lnt[:, 2, :], OP.subtract)
                c1 = vTS(lnt[:, 1, :], lnt[:, 1, :], EPS, OP.add)
                E.wait(sc, "sc", c1)
                E.inc(sc.activation(lnt[:, 1, :], lnt[:, 1, :], AF.Ln), "sS")
                c3 = E.inc(sc.activation(lnt[:, 2, :], lnt[:, 1, :], AF.Exp,
                                         scale=-0.5), "sS")
                E.wait(ve, "ve", c3)
                vTT(lnt[:, 3, :], lnt[:, 0, :], lnt[:, 2, :], OP.mult)
                for a in range(NI):
                    vTT(xnblk[:, a, :], xblk[:, a, :], lnt[:, 2, :], OP.mult)
                    xnt = vTT(xnblk[:, a, :], xnblk[:, a, :], lnt[:, 3, :],
                              OP.subtract)
                    E.wait(sc, "sc", xnt)
                    E.inc(sc.dma_start(out=xnS[:, a, c0:c0 + LNB],
                                       in_=xnblk[:, a, 1:LNB + 1]), "dC", 16)
                    dt_ = vTT(dlt[:], xnblk[:, a, 1:LNB + 1],
                              xnblk[:, a, 0:LNB], OP.subtract)
                    for mi in range(4):
                        pb = mi % 2
                        E.wait(ve, "ve", mspill[pb])
                        mt = E.inc(ve.scalar_tensor_tensor(
                            mixb[:, pb, :], dlt[:], cols[:, a, 2 + mi:3 + mi],
                            xnblk[:, a, 0:LNB], OP.mult, OP.add), "vS")
                        q, qn, qs = ((sy, "sy", "dS") if pb == 0 else
                                     (sc, "sc", "dC"))
                        E.wait(q, qn, mt)
                        mspill[pb] = E.inc(q.dma_start(
                            out=mS[mi][:, a, c0:c0 + LNB], in_=mixb[:, pb, :]),
                            qs, 16)
            E.barrier(ENGS)

        with (
            nc.sbuf_tensor("xn2", [128, NI, 512], BF16) as xn2,
            nc.sbuf_tensor("rt", [128, 4, 8], F32) as rt,
            nc.sbuf_tensor("hwb", [128, 128], BF16) as hwb,
            nc.psum_tensor("lgp", [128, 8], F32) as lgp,
            nc.psum_tensor("hp", [128, 128], F32) as hp,
            nc.psum_tensor("htp", [128, 128], BF16) as htp,
        ):
            for blk in range(4):
                tau0 = blk * 512
                E.barrier(ENGS)
                lds = []
                for a in range(NI):
                    lds.append(E.inc(sy.dma_start(
                        out=xn2[:, a, :], in_=xnS[:, a, tau0:tau0 + 512]),
                        "dS", 16))
                for cc in range(4):
                    chunk = blk * 4 + cc
                    off = cc * 128
                    E.wait(te, "te", E.tok("vS"), E.tok("sS"))
                    for a in range(NI):
                        E.wait(te, "te", lds[a])
                        E.inc(te.matmul(lgp[:], xn2[:, a, off:off + 128],
                                        wgate_s[:, a, :], start=(a == 0),
                                        stop=(a == NI - 1)), "tS")
                        hm = E.inc(te.matmul(hp[:], xn2[:, a, off:off + 128],
                                             aT_s[:, a, :], start=(a == 0),
                                             stop=(a == NI - 1)), "tS")
                    E.wait(ve, "ve", hm)
                    r1 = E.inc(ve.reduce_max(rt[:, 0, 0:1], lgp[:], axis=AX),
                               "vS")
                    E.wait(ve, "ve", r1)
                    eqt = vTS(rt[:, 1, :], lgp[:], rt[:, 0, 0:1], OP.is_equal)
                    E.wait(ve, "ve", eqt)
                    e2 = E.inc(ve.scalar_tensor_tensor(
                        rt[:, 2, :], rt[:, 1, :], -1e30, lgp[:], OP.mult,
                        OP.add), "vS")
                    E.wait(ve, "ve", e2)
                    r2 = E.inc(ve.reduce_max(rt[:, 0, 1:2], rt[:, 2, :],
                                             axis=AX), "vS")
                    ng = vTS(rt[:, 0, 2:3], rt[:, 0, 0:1], -1.0, OP.mult)
                    E.wait(sc, "sc", ng)
                    ex = E.inc(sc.activation(rt[:, 3, :], lgp[:], AF.Exp,
                                             bias=rt[:, 0, 2:3]), "sS")
                    E.wait(ve, "ve", r2)
                    sl2 = vTS(rt[:, 1, :], lgp[:], rt[:, 0, 1:2], OP.is_ge)
                    E.wait(ve, "ve", ex, sl2)
                    nm = vTT(rt[:, 3, :], rt[:, 3, :], rt[:, 1, :], OP.mult)
                    E.wait(ve, "ve", nm)
                    dn = E.inc(ve.reduce_sum(rt[:, 0, 3:4], rt[:, 3, :],
                                             axis=AX), "vS")
                    E.wait(ve, "ve", dn)
                    rc = E.inc(ve.reciprocal(rt[:, 0, 4:5], rt[:, 0, 3:4]),
                               "vS")
                    E.wait(ve, "ve", rc)
                    rw = vTS(route[:, chunk, :], rt[:, 3, :], rt[:, 0, 4:5],
                             OP.mult)
                    E.wait(ve, "ve", rw)
                    hw = vTT(hwb[:].rearrange("p (e r) -> p e r", e=8),
                             hp[:].rearrange("p (e r) -> p e r", e=8),
                             route[:, chunk, :, None].broadcast_to(
                                 (128, 8, 16)), OP.mult)
                    E.wait(te, "te", hw)
                    tp = E.inc(te.transpose(htp[:], hwb[:], ident[:]), "tS")
                    E.wait(ve, "ve", tp)
                    vCP(hwfm[:, chunk * 128:(chunk + 1) * 128], htp[:])
            if dbg:
                E.barrier(ENGS)
                E.inc(sy.dma_start(out=hwdbg[:, :], in_=hwfm[:]), "dS", 16)
                E.inc(sy.dma_start(out=rtdbg[:, :, :], in_=route[:]), "dS", 16)
            E.barrier(ENGS)

        def proj(tag, win, mixsrc, dstS, act=None):
            with (
                nc.sbuf_tensor(f"mixh{tag}", [128, NI, 1024], BF16) as mixh,
                nc.sbuf_tensor(f"wtile{tag}", [128, 2, CH], BF16) as wtile,
                nc.sbuf_tensor(f"evb{tag}", [128, 2, 512], BF16) as evb,
                nc.psum_tensor(f"pp{tag}", [128, 2, 512], F32) as pp,
            ):
                prev_ev = [None, None]
                prev_sp = [None, None]
                for h in range(2):
                    E.wait(sy, "sy", E.tok("tS"))
                    ml = []
                    for a in range(NI):
                        ml.append(E.inc(sy.dma_start(
                            out=mixh[:, a, :],
                            in_=mixsrc[:, a, h * 1024:(h + 1) * 1024]),
                            "dS", 16))
                    wl = [None] * NI
                    E.wait(gp, "gp", E.tok("tS"))
                    wl[0] = E.inc(gp.dma_start(out=wtile[:, 0, :], in_=win[0]),
                                  "dG", 16)
                    wl[1] = E.inc(gp.dma_start(out=wtile[:, 1, :], in_=win[1]),
                                  "dG", 16)
                    for i in range(NI):
                        bufi = i % 2
                        E.barrier(ENGS)
                        E.wait(te, "te", wl[i], prev_ev[0], prev_ev[1])
                        for a in range(NI):
                            E.wait(te, "te", ml[a])
                            for ts2 in range(2):
                                E.inc(te.matmul(
                                    pp[:, ts2, :],
                                    wtile[:, bufi, a * 128:(a + 1) * 128],
                                    mixh[:, a, ts2 * 512:(ts2 + 1) * 512],
                                    start=(a == 0), stop=(a == NI - 1)), "tS")
                        mmt = E.tok("tS")
                        if i + 2 < NI:
                            E.wait(gp, "gp", mmt)
                            wl[i + 2] = E.inc(gp.dma_start(
                                out=wtile[:, bufi, :], in_=win[i + 2]), "dG", 16)
                        for ts2 in range(2):
                            tau0 = h * 1024 + ts2 * 512
                            c0 = tau0 // 128
                            if act is None:
                                E.wait(ve, "ve", mmt, prev_sp[ts2])
                                ev = vCP(evb[:, ts2, :], pp[:, ts2, :])
                                eng, en, qs = sc, "sc", "dC"
                            else:
                                E.wait(sc, "sc", mmt, prev_sp[ts2])
                                ev = E.inc(sc.activation(
                                    evb[:, ts2, :], pp[:, ts2, :], act), "sS")
                                eng, en, qs = sy, "sy", "dS"
                            prev_ev[ts2] = ev
                            E.wait(eng, en, ev)
                            prev_sp[ts2] = E.inc(eng.dma_start(
                                out=dstS[:, c0:c0 + 4, i, :],
                                in_=evb[:, ts2, :].rearrange(
                                    "p (c t) -> p c t", c=4)), qs, 16)
            E.barrier(ENGS)

        proj("k", wins["wk"], mS[0], kS)
        if dbg:
            E.inc(sy.dma_start(out=kdbg[:, :, :, :], in_=kS[:, :, :, :]),
                  "dS", 16)
            E.barrier(ENGS)
        proj("v", wins["wv"], mS[1], vS)

        with (
            nc.sbuf_tensor("kfa", [128, NI, 128], BF16) as kfa,
            nc.sbuf_tensor("vfa", [128, NI, 128], BF16) as vfa,
            nc.sbuf_tensor("khf", [128, NI, 128], BF16) as khf,
            nc.sbuf_tensor("ktm", [128, NI, 128], BF16) as ktm,
            nc.sbuf_tensor("vtm", [128, NI, 128], BF16) as vtm,
            nc.psum_tensor("tpp", [128, 2, 128], BF16) as tpp,
            nc.psum_tensor("sdp", [128, NI, 64], F32) as sdp,
        ):
            prev_state = E.tok("vS")
            for c in range(NCHK):
                E.barrier(ENGS)
                E.wait(sy, "sy", E.tok("tS"))
                lk = E.inc(sy.dma_start(out=kfa[:], in_=kS[:, c, :, :]),
                           "dS", 16)
                lv = E.inc(sy.dma_start(out=vfa[:], in_=vS[:, c, :, :]),
                           "dS", 16)
                E.wait(ve, "ve", lk, E.tok("tS"))
                kh = vTT(khf[:].rearrange("p i (a t) -> p i a t", t=SUB),
                         kfa[:].rearrange("p i (a t) -> p i a t", t=SUB),
                         Dkf[:, :, None, :].broadcast_to(
                             (128, NI, 128 // SUB, SUB)), OP.mult)
                evt = [None, None]
                for i in range(NI):
                    par = i % 2
                    E.wait(te, "te", kh, lv, evt[par])
                    tk = E.inc(te.transpose(tpp[:, par, :], khf[:, i, :],
                                            ident[:]), "tS")
                    E.wait(ve, "ve", tk)
                    e1 = vCP(ktm[:, i, :], tpp[:, par, :])
                    E.wait(te, "te", e1)
                    tv = E.inc(te.transpose(tpp[:, par, :], vfa[:, i, :],
                                            ident[:]), "tS")
                    E.wait(ve, "ve", tv)
                    evt[par] = vCP(vtm[:, i, :], tpp[:, par, :])
                tdone = (evt[0], evt[1])
                for m in range(4):
                    M = c * 4 + m
                    E.wait(gp, "gp", prev_state)
                    spl = E.inc(gp.dma_start(
                        out=stS[:, M, :],
                        in_=Sst[:].rearrange("p i e -> p (i e)")), "dG", 16)
                    E.wait(te, "te", tdone[0], tdone[1], prev_state)
                    for i in range(NI):
                        for hh in range(2):
                            E.inc(te.matmul(
                                sdp[hh * 64:(hh + 1) * 64, i, :],
                                ktm[m * 32:(m + 1) * 32, i,
                                    hh * 64:(hh + 1) * 64],
                                vtm[m * 32:(m + 1) * 32, i,
                                    hh * 64:(hh + 1) * 64],
                                start=True, stop=True,
                                tile_position=(m * 32, hh * 64)), "tS")
                    um = E.tok("tS")
                    E.wait(ve, "ve", um, spl)
                    vTT(Sst[:], Sst[:],
                        wt32[:, :, None].broadcast_to((128, NI, 64)), OP.mult)
                    prev_state = vTT(Sst[:], Sst[:], sdp[:], OP.add)
            E.wait(gp, "gp", prev_state)
            stg = E.inc(gp.dma_start(
                out=ccin[:, :], in_=Sst[:].rearrange("p i e -> p (i e)")),
                "dG", 16)
            E.wait(gp, "gp", stg)
            E.inc(gp.collective_compute(
                "AllGather", OP.bypass,
                replica_groups=[[0, 1, 2, 3], [4, 5, 6, 7]],
                ins=[ccin.ap().opt()], outs=[ccout.ap().opt()]), "cS")
            E.barrier(ENGS)

        proj("r", wins["wr"], mS[2], rS)
        proj("g", wins["wg"], mS[3], gS, act=AF.Silu)

        with nc.sbuf_tensor("agb", [128, NI, 64], BF16) as agb:
            z0 = E.inc(ve.memset(Sin[:], 0.0), "vS")
            E.wait(sy, "sy", E.tok("cS"))
            for j in range(4):
                lj = E.inc(sy.dma_start(
                    out=agb[:].rearrange("p i e -> p (i e)"),
                    in_=ccout[j, :, :]), "dS", 16)
                E.wait(ve, "ve", lj, z0)
                vTT(agb[:], agb[:],
                    cinc[:, :, j:j + 1].broadcast_to((128, NI, 64)), OP.mult)
                z0 = vTT(Sin[:], Sin[:], agb[:], OP.add)
                E.wait(sy, "sy", z0)
            if dbg:
                E.inc(sy.dma_start(out=ccdbg[:, :, :], in_=ccout[:, :, :]),
                      "dS", 16)
            E.barrier(ENGS)

        with (
            nc.sbuf_tensor("kfb", [128, NI, 128], BF16) as kfb,
            nc.sbuf_tensor("rfb", [128, NI, 128], BF16) as rfb,
            nc.sbuf_tensor("vpd", [128, NI, 160], BF16) as vpd,
            nc.sbuf_tensor("Bb", [128, NI, 160], BF16) as Bb,
            nc.sbuf_tensor("sbb", [128, NI, 128], BF16) as sbb,
            nc.sbuf_tensor("rsc", [128, NI, 128], BF16) as rsc,
            nc.sbuf_tensor("stb", [128, 4, NI, 64], BF16) as stb,
            nc.sbuf_tensor("seftmp", [128, NI, 64], BF16) as seftmp,
            nc.sbuf_tensor("sef", [128, 4, NI, 64], BF16) as sef,
            nc.sbuf_tensor("Osb", [128, NI, 128], F32) as Osb,
            nc.sbuf_tensor("appt", [128, 8, 128], F32) as appt,
            nc.psum_tensor("abp", [128, 2, 8, 128], F32) as abp,
            nc.psum_tensor("oip", [128, 2, 8, 128], F32) as oip,
        ):
            E.inc(ve.memset(vpd[:, :, 0:32], 0.0), "vS")
            pz2 = E.inc(ve.memset(Bb[:, :, 0:32], 0.0), "vS")
            prev_osp = None
            for c in range(NCHK):
                E.barrier(ENGS)
                E.wait(sy, "sy", E.tok("tS"), E.tok("vS"))
                lk = E.inc(sy.dma_start(out=kfb[:], in_=kS[:, c, :, :]),
                           "dS", 16)
                lr = E.inc(sy.dma_start(out=rfb[:], in_=rS[:, c, :, :]),
                           "dS", 16)
                lv = E.inc(sy.dma_start(out=vpd[:, :, 32:160],
                                        in_=vS[:, c, :, :]), "dS", 16)
                lst = E.inc(sy.dma_start(
                    out=stb[:].rearrange("p m i e -> p (m i e)"),
                    in_=stS[:, 4 * c:4 * c + 4, :].rearrange(
                        "p m f -> p (m f)")), "dS", 16)
                seft = []
                E.wait(ve, "ve", lst, E.tok("tS"))
                for m in range(4):
                    M = 4 * c + m
                    vTT(seftmp[:], Sin[:],
                        wpow[:, :, M:M + 1].broadcast_to((128, NI, 64)),
                        OP.mult)
                    seft.append(vTT(sef[:, m, :, :], seftmp[:], stb[:, m, :, :],
                                    OP.add))
                E.wait(ve, "ve", lr)
                rsct = vTT(rsc[:].rearrange("p i (a t) -> p i a t", t=SUB),
                           rfb[:].rearrange("p i (a t) -> p i a t", t=SUB),
                           Dq[:, :, None, :].broadcast_to(
                               (128, NI, 128 // SUB, SUB)), OP.mult)
                E.wait(ve, "ve", lk, lv)
                vTT(Bb[:, :, 32:160], kfb[:],
                    cols[:, :, 1][:, :, None].broadcast_to((128, NI, 128)),
                    OP.mult)
                st0 = vTT(sbb[:], Bb[:, :, 32:160], rfb[:], OP.mult)
                sprev = st0
                binit_done = False
                o_init = [False] * 4
                prev_app = [E.tok("vS"), E.tok("vS")]
                for dl in range(BAND):
                    if dl >= 1:
                        E.wait(ve, "ve", E.tok("tS"))
                        if dl == 1:
                            E.wait(ve, "ve", pz2)
                            vCP(Bb[:, :, 32:160], kfb[:])
                        else:
                            vTT(Bb[:, :, 32:160], Bb[:, :, 32:160],
                                wcol[:, :, None].broadcast_to((128, NI, 128)),
                                OP.mult)
                        E.inc(ve.memset(
                            Bb[:, :, 32:160].rearrange(
                                "p i (b t) -> p i b t", t=SUB)
                            [:, :, :, SUB - dl:SUB - dl + 1], 0.0), "vS")
                        sprev = vTT(sbb[:], rfb[:], Bb[:, :, 32 - dl:160 - dl],
                                    OP.mult)
                    for g in range(4):
                        par = g % 2
                        E.wait(te, "te", sprev, prev_app[par])
                        rm = None
                        for ii in range(8):
                            i = g * 8 + ii
                            rm = E.inc(te.matmul(
                                abp[:, par, ii, :], indbd[:], sbb[:, i, :],
                                start=True, stop=True), "tS")
                        E.wait(ve, "ve", rm, prev_osp)
                        vTT(appt[:], abp[:, par, :, :],
                            vpd[:, g * 8:g * 8 + 8, 32 - dl:160 - dl], OP.mult)
                        if not o_init[g]:
                            prev_app[par] = vCP(Osb[:, g * 8:g * 8 + 8, :],
                                                appt[:])
                            o_init[g] = True
                        else:
                            prev_app[par] = vTT(Osb[:, g * 8:g * 8 + 8, :],
                                                Osb[:, g * 8:g * 8 + 8, :],
                                                appt[:], OP.add)
                for g in range(4):
                    par = g % 2
                    E.wait(te, "te", rsct, seft[3], prev_app[par])
                    ot = None
                    for m in range(4):
                        for ii in range(8):
                            i = g * 8 + ii
                            for hh in range(2):
                                ot = E.inc(te.matmul(
                                    oip[hh * 64:(hh + 1) * 64, par, ii,
                                        m * 32:(m + 1) * 32],
                                    sef[hh * 64:(hh + 1) * 64, m, i, :],
                                    rsc[hh * 64:(hh + 1) * 64, i,
                                        m * 32:(m + 1) * 32],
                                    start=True, stop=True,
                                    tile_position=(hh * 64, hh * 64)), "tS")
                    E.wait(ve, "ve", ot)
                    prev_app[par] = vTT(Osb[:, g * 8:g * 8 + 8, :],
                                        Osb[:, g * 8:g * 8 + 8, :],
                                        oip[:, par, :, :], OP.add)
                E.wait(sc, "sc", prev_app[0], prev_app[1])
                prev_osp = E.inc(sc.dma_start(
                    out=OSs[:, c, :, :],
                    in_=Osb[:].rearrange("p i t -> p (i t)").rearrange(
                        "p (i t) -> p i t", t=128)), "dC", 16)
            E.barrier(ENGS)

        with (
            nc.sbuf_tensor("Og", [128, NI, 128], F32) as Og,
            nc.sbuf_tensor("Obf", [128, NI, 128], BF16) as Obf,
            nc.sbuf_tensor("Osq", [128, NI, 128], BF16) as Osq,
            nc.sbuf_tensor("gfb", [128, NI, 128], BF16) as gfb,
            nc.sbuf_tensor("mnb", [128, NI, 128], F32) as mnb,
            nc.sbuf_tensor("msb", [128, NI, 128], F32) as msb,
            nc.sbuf_tensor("gnt", [128, NI, 128], F32) as gnt,
            nc.psum_tensor("gsp", [128, 2, 4, 128], F32) as gsp,
        ):
            for c in range(NCHK):
                E.barrier(ENGS)
                E.wait(sy, "sy", E.tok("vS"), E.tok("tS"), E.tok("dC"))
                lo = E.inc(sy.dma_start(
                    out=Og[:].rearrange("p i t -> p (i t)"),
                    in_=OSs[:, c, :, :].rearrange("p i t -> p (i t)")),
                    "dS", 16)
                lg = E.inc(sy.dma_start(out=gfb[:], in_=gS[:, c, :, :]),
                           "dS", 16)
                E.wait(ve, "ve", lo)
                ob = vCP(Obf[:], Og[:])
                E.wait(sc, "sc", ob)
                osq = E.inc(sc.activation(Osq[:], Obf[:], AF.Square), "sS")
                gst = [None, None]
                for gg in range(16):
                    par = gg % 2
                    E.wait(te, "te", ob, osq, gst[par])
                    for ii in range(2):
                        i = gg * 2 + ii
                        E.inc(te.matmul(gsp[:, par, 2 * ii, :], indgn[:],
                                        Obf[:, i, :], start=True, stop=True),
                              "tS")
                        gm = E.inc(te.matmul(gsp[:, par, 2 * ii + 1, :],
                                             indgn[:], Osq[:, i, :],
                                             start=True, stop=True), "tS")
                    E.wait(ve, "ve", gm)
                    vCP(mnb[:, gg * 2:gg * 2 + 2, :], gsp[:, par, 0::2, :])
                    gst[par] = vCP(msb[:, gg * 2:gg * 2 + 2, :],
                                   gsp[:, par, 1::2, :])
                vTT(gnt[:], mnb[:], mnb[:], OP.mult)
                vTT(gnt[:], msb[:], gnt[:], OP.subtract)
                c1 = vTS(gnt[:], gnt[:], EPS, OP.add)
                E.wait(sc, "sc", c1)
                E.inc(sc.activation(gnt[:], gnt[:], AF.Ln), "sS")
                c3 = E.inc(sc.activation(gnt[:], gnt[:], AF.Exp, scale=-0.5),
                           "sS")
                E.wait(ve, "ve", c3, lg)
                vTT(Og[:], Og[:], mnb[:], OP.subtract)
                vTT(Og[:], Og[:], gnt[:], OP.mult)
                vTT(Og[:], Og[:],
                    cols[:, :, 6][:, :, None].broadcast_to((128, NI, 128)),
                    OP.mult)
                vTT(Og[:], Og[:],
                    cols[:, :, 7][:, :, None].broadcast_to((128, NI, 128)),
                    OP.add)
                gngt = vTT(Obf[:], Og[:], gfb[:], OP.mult)
                E.wait(sc, "sc", gngt)
                E.inc(sc.dma_start(out=gngS[:, c, :, :],
                                   in_=Obf[:].rearrange(
                                       "p i t -> p i t")), "dC", 16)
            E.barrier(ENGS)

        with (
            nc.sbuf_tensor("xnh", [128, NI, 1024], BF16) as xnh,
            nc.sbuf_tensor("wtile6", [128, 2, CH], BF16) as wtile,
            nc.sbuf_tensor("evf", [128, 2, 512], F32) as evf,
            nc.psum_tensor("pp6", [128, 2, 512], F32) as pp6,
        ):
            prev_ev = [None, None]
            prev_sp = [None, None]
            for h in range(2):
                E.wait(sy, "sy", E.tok("tS"))
                ml = []
                for a in range(NI):
                    ml.append(E.inc(sy.dma_start(
                        out=xnh[:, a, :],
                        in_=xnS[:, a, h * 1024:(h + 1) * 1024]), "dS", 16))
                wl = [None] * NI
                E.wait(gp, "gp", E.tok("tS"))
                wl[0] = E.inc(gp.dma_start(out=wtile[:, 0, :], in_=wins["wb"][0]),
                              "dG", 16)
                wl[1] = E.inc(gp.dma_start(out=wtile[:, 1, :], in_=wins["wb"][1]),
                              "dG", 16)
                for oc in range(NI):
                    bufi = oc % 2
                    E.barrier(ENGS)
                    E.wait(te, "te", wl[oc], prev_ev[0], prev_ev[1])
                    for a in range(NI):
                        E.wait(te, "te", ml[a])
                        for ts2 in range(2):
                            E.inc(te.matmul(
                                pp6[:, ts2, :],
                                wtile[:, bufi, a * 128:(a + 1) * 128],
                                xnh[:, a, ts2 * 512:(ts2 + 1) * 512],
                                start=(a == 0), stop=False), "tS")
                    for ts2 in range(2):
                        tau0 = h * 1024 + ts2 * 512
                        E.inc(te.matmul(
                            pp6[:, ts2, :], bb_s[0:1, oc * 128:(oc + 1) * 128],
                            onesr[0:1, :], start=False, stop=False), "tS")
                        E.inc(te.matmul(
                            pp6[:, ts2, :], bflat_s[:, oc * 128:(oc + 1) * 128],
                            hwfm[:, tau0:tau0 + 512], start=False, stop=True),
                            "tS")
                    mmt = E.tok("tS")
                    if oc + 2 < NI:
                        E.wait(gp, "gp", mmt)
                        wl[oc + 2] = E.inc(gp.dma_start(
                            out=wtile[:, bufi, :], in_=wins["wb"][oc + 2]),
                            "dG", 16)
                    for ts2 in range(2):
                        tau0 = h * 1024 + ts2 * 512
                        E.wait(ve, "ve", mmt, prev_sp[ts2])
                        ev = vCP(evf[:, ts2, :], pp6[:, ts2, :])
                        prev_ev[ts2] = ev
                        E.wait(sc, "sc", ev)
                        prev_sp[ts2] = E.inc(sc.dma_start(
                            out=outP[:, oc, tau0:tau0 + 512],
                            in_=evf[:, ts2, :]), "dC", 16)
            E.barrier(ENGS)

        with (
            nc.sbuf_tensor("gnh", [128, NI, 1024], BF16) as gnh,
            nc.sbuf_tensor("wtile7", [128, 2, CH], BF16) as wtile,
            nc.sbuf_tensor("pstr", [128, 2, 512], F32) as pstr,
            nc.sbuf_tensor("xstr", [128, 2, 512], F32) as xstr,
            nc.sbuf_tensor("ofin", [128, 2, 512], F32) as ofin,
            nc.psum_tensor("pp7", [128, 2, 512], F32) as pp7,
        ):
            prev_ev = [None, None]
            prev_sp = [None, None]
            for h in range(2):
                E.wait(sy, "sy", E.tok("tS"))
                lgn = E.inc(sy.dma_start(
                    out=gnh[:].rearrange("p i t -> p (i t)"),
                    in_=gngS[:, 8 * h:8 * h + 8, :, :].rearrange(
                        "p c i t -> p (c i t)")), "dS", 16)
                wl = [None] * NI
                E.wait(gp, "gp", E.tok("tS"))
                wl[0] = E.inc(gp.dma_start(out=wtile[:, 0, :], in_=wins["wo"][0]),
                              "dG", 16)
                wl[1] = E.inc(gp.dma_start(out=wtile[:, 1, :], in_=wins["wo"][1]),
                              "dG", 16)
                gv = gnh[:].rearrange("p i t -> p (i t)").rearrange(
                    "p (c i t) -> p c i t", c=8, i=NI)
                for oc in range(NI):
                    bufi = oc % 2
                    E.barrier(ENGS)
                    E.wait(te, "te", wl[oc], lgn, prev_ev[0], prev_ev[1])
                    for a in range(NI):
                        for ts2 in range(2):
                            E.inc(te.matmul(
                                pp7[:, ts2, :],
                                wtile[:, bufi, a * 128:(a + 1) * 128],
                                gv[:, 4 * ts2:4 * ts2 + 4, a, :],
                                start=(a == 0), stop=(a == NI - 1)), "tS")
                    mmt = E.tok("tS")
                    if oc + 2 < NI:
                        E.wait(gp, "gp", mmt)
                        wl[oc + 2] = E.inc(gp.dma_start(
                            out=wtile[:, bufi, :], in_=wins["wo"][oc + 2]),
                            "dG", 16)
                    for ts2 in range(2):
                        tau0 = h * 1024 + ts2 * 512
                        E.wait(sy, "sy", prev_sp[ts2])
                        lp = E.inc(sy.dma_start(
                            out=pstr[:, ts2, :],
                            in_=outP[:, oc, tau0:tau0 + 512]), "dS", 16)
                        lx = E.inc(sy.dma_start(
                            out=xstr[:, ts2, :],
                            in_=xT[:, oc, tau0 + 1:tau0 + 513]), "dS", 16)
                        E.wait(ve, "ve", mmt, lp, lx, prev_sp[ts2])
                        vTT(ofin[:, ts2, :], pp7[:, ts2, :], pstr[:, ts2, :], OP.add)
                        ev = vTT(ofin[:, ts2, :], ofin[:, ts2, :], xstr[:, ts2, :],
                                 OP.add)
                        prev_ev[ts2] = ev
                        E.wait(sc, "sc", ev)
                        prev_sp[ts2] = E.inc(sc.dma_start(
                            out=outF[:, oc, tau0:tau0 + 512],
                            in_=ofin[:, ts2, :]), "dC", 16)
            E.barrier(ENGS)
    return nc


def _host_prep(inputs):
    bf = ml_dtypes.bfloat16
    x = np.asarray(inputs["L0_attn_hidden"], np.float32)

    def tile_w(w):
        return np.ascontiguousarray(
            np.asarray(w, np.float32).T.reshape(NI, 128, NI, 128)
            .transpose(2, 1, 0, 3).reshape(NI, 128, CH))

    wk = tile_w(inputs["L0_attn_w_key"])
    wv = tile_w(inputs["L0_attn_w_value"])
    wr = tile_w(inputs["L0_attn_w_receptance"])
    wg = tile_w(inputs["L0_attn_w_gate"])
    wo = tile_w(inputs["L0_attn_w_output"])
    wb = tile_w(inputs["L0_ffn_W_base"])
    wgate = np.ascontiguousarray(
        np.asarray(inputs["L0_ffn_W_gate"], np.float32).T
        .reshape(NI, 128, 8).transpose(1, 0, 2))
    A2 = np.asarray(inputs["L0_ffn_A"], np.float32).reshape(128, CH)
    aT = np.ascontiguousarray(A2.T.reshape(NI, 128, 128).transpose(1, 0, 2))
    bflat = np.ascontiguousarray(
        np.asarray(inputs["L0_ffn_B"], np.float32).transpose(0, 2, 1)
        .reshape(128, CH))

    def col(v):
        return np.asarray(v, np.float32).reshape(NI, 128).T

    cols = np.stack([
        col(inputs["L0_attn_time_decay"].reshape(CH)),
        col(inputs["L0_attn_time_faaaa"].reshape(CH)),
        col(inputs["L0_attn_time_mix_key"]),
        col(inputs["L0_attn_time_mix_value"]),
        col(inputs["L0_attn_time_mix_receptance"]),
        col(inputs["L0_attn_time_mix_gate"]),
        col(inputs["L0_attn_gn_weight"]),
        col(inputs["L0_attn_gn_bias"]),
    ], axis=2)
    cols = np.ascontiguousarray(cols)
    bb = np.asarray(inputs["L0_ffn_b_base"], np.float32).reshape(1, CH)

    ident = np.eye(128, dtype=bf)
    onesc = np.full((128, 128), 1.0 / CH, dtype=bf)
    blk = np.zeros((128, 128), np.float32)
    blk[0:64, 0:64] = 1.0
    blk[64:128, 64:128] = 1.0
    indbd = blk.astype(bf)
    indgn = (blk / 64.0).astype(bf)
    onesr = np.ones((1, 512), dtype=bf)
    tmod = np.arange(SUB, dtype=np.float32) % SUB
    EQ = np.broadcast_to(tmod, (128, SUB)).copy()
    EK = np.broadcast_to(SUB - 1 - tmod, (128, SUB)).astype(np.float32).copy()
    EM = np.broadcast_to(32.0 * np.arange(64, dtype=np.float32),
                         (128, 64)).copy()

    shared = dict(wk=wk, wv=wv, wr=wr, wg=wg, wo=wo, wb=wb, wgate=wgate,
                  aT=aT, bflat=bflat, cols=cols, bb=bb, identI=ident,
                  onescI=onesc, indgnI=indgn, indbdI=indbd, onesrI=onesr,
                  EQI=EQ, EKI=EK, EMI=EM)

    in_maps = []
    for core in range(8):
        b = core // 4
        q = core % 4
        seg = np.zeros((TOK + 1, CH), np.float32)
        seg[1:] = x[b, q * TOK:(q + 1) * TOK]
        if q > 0:
            seg[0] = x[b, q * TOK - 1]
        xTc = np.ascontiguousarray(
            seg.T.reshape(NI, 128, TOK + 1).transpose(1, 0, 2))
        ce = np.zeros(4, np.float32)
        for j in range(4):
            ce[j] = 2048.0 * (q - 1 - j) if j < q else 1e9
        cexp = np.broadcast_to(ce, (128, 4)).copy()
        m = dict(shared)
        m["xT"] = xTc
        m["cexpI"] = cexp
        in_maps.append(m)
    return in_maps


def kernel(**inputs):
    in_maps = _host_prep(inputs)
    nc = build_nc()
    res = run_bass_kernel_spmd(nc, in_maps, core_ids=list(range(8)))
    out = np.empty((B, S, CH), np.float32)
    for core in range(8):
        b, q = core // 4, core % 4
        of = np.asarray(res.results[core]["outF"])
        out[b, q * TOK:(q + 1) * TOK] = (
            of.transpose(1, 0, 2).reshape(CH, TOK).T)
    return out


# version 05 (final)
# speedup vs baseline: 1.1266x; 1.1266x over previous
import numpy as np
import ml_dtypes
from contextlib import ExitStack

import concourse.bass as bass
import concourse.mybir as mybir
from concourse.bass_utils import run_bass_kernel_spmd

F32 = mybir.dt.float32
BF16 = mybir.dt.bfloat16
AF = mybir.ActivationFunctionType
OP = mybir.AluOpType
AX = mybir.AxisListType.X

B, S, CH = 2, 8192, 4096
TOK = 2048
NI = 32
NCHK = 16
SUB = 32
LNB = 256
EPS = 1e-5
BAND = 32


class Emit:
    def __init__(self, nc, stack):
        self.nc = nc
        self.cnt = {}
        self.sem = {}
        self.wm = {}
        for s in ["dS", "dG", "dC", "tS", "vS", "sS", "cS"]:
            self.sem[s] = stack.enter_context(nc.semaphore(s))
            self.cnt[s] = 0
            self.wm[s] = {}

    def inc(self, ins, s, n=1):
        ins.then_inc(self.sem[s], n)
        self.cnt[s] += n
        return (s, self.cnt[s])

    def tok(self, s):
        return (s, self.cnt[s])

    def wait(self, eng, ename, *toks):
        for t in toks:
            if t is None:
                continue
            s, v = t
            if v <= 0 or self.wm[s].get(ename, 0) >= v:
                continue
            eng.wait_ge(self.sem[s], v)
            self.wm[s][ename] = v

    def barrier(self, engs):
        toks = [self.tok(s) for s in self.sem]
        for eng, en in engs:
            self.wait(eng, en, *toks)


def build_nc(dbg=False):
    nc = bass.Bass("TRN2", target_bir_lowering=False)
    d = nc.dram_tensor

    xT = d("xT", [128, NI, TOK + 1], F32, kind="ExternalInput").ap()
    wins = {}
    for nm in ["wk", "wv", "wr", "wg", "wo", "wb"]:
        wins[nm] = d(nm, [NI, 128, CH], F32, kind="ExternalInput").ap()
    wgate = d("wgate", [128, NI, 8], F32, kind="ExternalInput").ap()
    aTin = d("aT", [128, NI, 128], F32, kind="ExternalInput").ap()
    bfin = d("bflat", [128, CH], F32, kind="ExternalInput").ap()
    cols_in = d("cols", [128, NI, 8], F32, kind="ExternalInput").ap()
    bb_in = d("bb", [1, CH], F32, kind="ExternalInput").ap()
    identI = d("identI", [128, 128], BF16, kind="ExternalInput").ap()
    onescI = d("onescI", [128, 128], BF16, kind="ExternalInput").ap()
    indgnI = d("indgnI", [128, 128], BF16, kind="ExternalInput").ap()
    indbdI = d("indbdI", [128, 128], BF16, kind="ExternalInput").ap()
    onesrI = d("onesrI", [1, 512], BF16, kind="ExternalInput").ap()
    EQI = d("EQI", [128, SUB], F32, kind="ExternalInput").ap()
    EKI = d("EKI", [128, SUB], F32, kind="ExternalInput").ap()
    EMI = d("EMI", [128, 64], F32, kind="ExternalInput").ap()
    cexpI = d("cexpI", [128, 4], F32, kind="ExternalInput").ap()
    outF = d("outF", [128, NI, TOK], F32, kind="ExternalOutput").ap()

    kw = dict(kind="ExternalOutput") if dbg else {}
    xnS = d("xnS", [128, NI, TOK], BF16, **kw)
    mS = [d(f"m{i}S", [128, NI, TOK], BF16, **kw) for i in range(4)]
    kS = d("kS", [128, NCHK, NI, 128], BF16, **kw)
    vS = d("vS", [128, NCHK, NI, 128], BF16, **kw)
    rS = d("rS", [128, NCHK, NI, 128], BF16, **kw)
    gS = d("gS", [128, NCHK, NI, 128], BF16, **kw)
    OSs = d("OSs", [128, NCHK, NI, 128], F32, **kw)
    gngS = d("gngS", [128, NCHK, NI, 128], BF16, **kw)
    outP = d("outP", [128, NI, TOK], F32, **kw)
    stS = d("stS", [128, 64, TOK], BF16, **kw)
    ccin = d("ccin", [128, TOK], BF16)
    ccout = d("ccout", [4, 128, TOK], BF16)
    ccdbg = d("ccdbg", [4, 128, TOK], BF16, **kw) if dbg else None
    kdbg = d("kdbg", [128, NCHK, NI, 128], BF16, **kw) if dbg else None
    hwdbg = d("hwdbg", [128, TOK], BF16, **kw) if dbg else None
    rtdbg = d("rtdbg", [128, NCHK, 8], F32, **kw) if dbg else None

    st = ExitStack()
    with st:
        E = Emit(nc, st)
        sb = lambda name, shape, dt: st.enter_context(
            nc.sbuf_tensor(name, shape, dt))

        cols = sb("colsb", [128, NI, 8], F32)
        lw = sb("lw", [128, NI], F32)
        wcol = sb("wcol", [128, NI], F32)
        wt32 = sb("wt32", [128, NI], F32)
        wpow = sb("wpow", [128, NI, 64], F32)
        cinc = sb("cinc", [128, NI, 4], F32)
        Dq = sb("Dq", [128, NI, SUB], BF16)
        Dkf = sb("Dkf", [128, NI, SUB], BF16)
        ident = sb("ident", [128, 128], BF16)
        onesc = sb("onesc", [128, 128], BF16)
        indgn = sb("indgn", [128, 128], BF16)
        indbd = sb("indbd", [128, 128], BF16)
        onesr = sb("onesr", [1, 512], BF16)
        wgate_s = sb("wgate_s", [128, NI, 8], BF16)
        aT_s = sb("aT_s", [128, NI, 128], BF16)
        bflat_s = sb("bflat_s", [128, CH], BF16)
        bb_s = sb("bb_s", [1, CH], BF16)
        route = sb("route", [128, NCHK, 8], F32)
        hwfm = sb("hwfm", [128, TOK], BF16)
        Sst = sb("Sst", [128, NI, 64], F32)
        Sin = sb("Sin", [128, NI, 64], F32)
        eq = sb("eqb", [128, SUB], F32)
        ek = sb("ekb", [128, SUB], F32)
        em = sb("emb", [128, 64], F32)
        cexp = sb("cexpb", [128, 4], F32)

        sy, gp, sc, ve, te = nc.sync, nc.gpsimd, nc.scalar, nc.vector, nc.tensor
        ENGS = [(sy, "sy"), (gp, "gp"), (sc, "sc"), (ve, "ve"), (te, "te")]

        def vTT(out, a, b, op):
            return E.inc(ve.tensor_tensor(out, a, b, op), "vS")

        def vTS(out, a, s1, op):
            return E.inc(ve.tensor_scalar(out, a, s1, None, op), "vS")

        def vCP(out, a):
            return E.inc(ve.tensor_copy(out, a), "vS")

        for dst, src in [(cols, cols_in), (ident, identI), (onesc, onescI),
                         (indgn, indgnI), (indbd, indbdI), (onesr, onesrI),
                         (eq, EQI), (ek, EKI), (em, EMI), (cexp, cexpI)]:
            E.inc(sy.dma_start(out=dst[:], in_=src), "dS", 16)
        ld0 = E.tok("dS")
        for dst, src in [(wgate_s, wgate), (aT_s, aTin), (bflat_s, bfin),
                         (bb_s, bb_in)]:
            E.inc(gp.dma_start(out=dst[:], in_=src), "dG", 16)
        ldg = E.tok("dG")

        E.wait(sc, "sc", ld0)
        t1 = E.inc(sc.activation(lw[:], cols[:, :, 0], AF.Exp), "sS")
        E.wait(ve, "ve", t1)
        t1 = vTS(lw[:], lw[:], -1.0, OP.mult)
        E.wait(sc, "sc", t1)
        E.inc(sc.activation(wcol[:], lw[:], AF.Exp), "sS")
        E.inc(sc.activation(wt32[:], lw[:], AF.Exp, scale=32.0), "sS")
        lwb = lambda n: lw[:, :, None].broadcast_to((128, NI, n))
        E.wait(ve, "ve", t1)
        t2 = vTT(wpow[:], em[:, None, :].broadcast_to((128, NI, 64)), lwb(64),
                 OP.mult)
        E.wait(sc, "sc", t2)
        E.inc(sc.activation(wpow[:], wpow[:], AF.Exp), "sS")
        t2 = vTT(cinc[:], cexp[:, None, :].broadcast_to((128, NI, 4)), lwb(4),
                 OP.mult)
        E.wait(sc, "sc", t2)
        E.inc(sc.activation(cinc[:], cinc[:], AF.Exp), "sS")
        t2 = vTT(Dq[:], eq[:, None, :].broadcast_to((128, NI, SUB)), lwb(SUB),
                 OP.mult)
        E.wait(sc, "sc", t2)
        E.inc(sc.activation(Dq[:], Dq[:], AF.Exp), "sS")
        t2 = vTT(Dkf[:], ek[:, None, :].broadcast_to((128, NI, SUB)), lwb(SUB),
                 OP.mult)
        E.wait(sc, "sc", t2)
        E.inc(sc.activation(Dkf[:], Dkf[:], AF.Exp), "sS")
        E.inc(ve.memset(Sst[:], 0.0), "vS")
        E.wait(ve, "ve", ldg)
        E.barrier(ENGS)

        with (
            nc.sbuf_tensor("xblk", [128, NI, LNB + 1], BF16) as xblk,
            nc.sbuf_tensor("xnblk", [128, NI, LNB + 1], BF16) as xnblk,
            nc.sbuf_tensor("xsq", [128, 2, LNB + 1], BF16) as xsq,
            nc.sbuf_tensor("dlt", [128, LNB], BF16) as dlt,
            nc.sbuf_tensor("mixb", [128, 2, LNB], BF16) as mixb,
            nc.sbuf_tensor("lnt", [128, 4, LNB + 1], F32) as lnt,
            nc.psum_tensor("lnp", [128, 2, 512], F32) as lnp,
        ):
            mspill = [None, None]
            for b in range(8):
                c0 = b * LNB
                E.wait(gp, "gp", E.tok("vS"), E.tok("sS"))
                ld = E.inc(gp.dma_start(out=xblk[:],
                                        in_=xT[:, :, c0:c0 + LNB + 1]), "dG", 16)
                E.wait(te, "te", E.tok("vS"))
                for a in range(NI):
                    E.wait(sc, "sc", ld, E.tok("tS") if a >= 2 else None)
                    sq = E.inc(sc.activation(xsq[:, a % 2, :], xblk[:, a, :],
                                             AF.Square), "sS")
                    E.wait(te, "te", ld, sq)
                    E.inc(te.matmul(lnp[:, 0, 0:LNB + 1], onesc[:], xblk[:, a, :],
                                    start=(a == 0), stop=(a == NI - 1)), "tS")
                    mmt = E.inc(te.matmul(lnp[:, 1, 0:LNB + 1], onesc[:], xsq[:, a % 2, :],
                                          start=(a == 0), stop=(a == NI - 1)),
                                "tS")
                E.wait(ve, "ve", mmt)
                vCP(lnt[:, 0, :], lnp[:, 0, 0:LNB + 1])
                vCP(lnt[:, 1, :], lnp[:, 1, 0:LNB + 1])
                vTT(lnt[:, 2, :], lnt[:, 0, :], lnt[:, 0, :], OP.mult)
                vTT(lnt[:, 1, :], lnt[:, 1, :], lnt[:, 2, :], OP.subtract)
                c1 = vTS(lnt[:, 1, :], lnt[:, 1, :], EPS, OP.add)
                E.wait(sc, "sc", c1)
                E.inc(sc.activation(lnt[:, 1, :], lnt[:, 1, :], AF.Ln), "sS")
                c3 = E.inc(sc.activation(lnt[:, 2, :], lnt[:, 1, :], AF.Exp,
                                         scale=-0.5), "sS")
                E.wait(ve, "ve", c3)
                vTT(lnt[:, 3, :], lnt[:, 0, :], lnt[:, 2, :], OP.mult)
                for a in range(NI):
                    vTT(xnblk[:, a, :], xblk[:, a, :], lnt[:, 2, :], OP.mult)
                    xnt = vTT(xnblk[:, a, :], xnblk[:, a, :], lnt[:, 3, :],
                              OP.subtract)
                    E.wait(sc, "sc", xnt)
                    E.inc(sc.dma_start(out=xnS[:, a, c0:c0 + LNB],
                                       in_=xnblk[:, a, 1:LNB + 1]), "dC", 16)
                    dt_ = vTT(dlt[:], xnblk[:, a, 1:LNB + 1],
                              xnblk[:, a, 0:LNB], OP.subtract)
                    for mi in range(4):
                        pb = mi % 2
                        E.wait(ve, "ve", mspill[pb])
                        mt = E.inc(ve.scalar_tensor_tensor(
                            mixb[:, pb, :], dlt[:], cols[:, a, 2 + mi:3 + mi],
                            xnblk[:, a, 0:LNB], OP.mult, OP.add), "vS")
                        q, qn, qs = ((sy, "sy", "dS") if pb == 0 else
                                     (sc, "sc", "dC"))
                        E.wait(q, qn, mt)
                        mspill[pb] = E.inc(q.dma_start(
                            out=mS[mi][:, a, c0:c0 + LNB], in_=mixb[:, pb, :]),
                            qs, 16)
            E.barrier(ENGS)

        with (
            nc.sbuf_tensor("xn2", [128, NI, 512], BF16) as xn2,
            nc.sbuf_tensor("rt", [128, 4, 8], F32) as rt,
            nc.sbuf_tensor("hwb", [128, 128], BF16) as hwb,
            nc.psum_tensor("lgp", [128, 8], F32) as lgp,
            nc.psum_tensor("hp", [128, 128], F32) as hp,
            nc.psum_tensor("htp", [128, 128], BF16) as htp,
        ):
            for blk in range(4):
                tau0 = blk * 512
                E.barrier(ENGS)
                lds = []
                for a in range(NI):
                    lds.append(E.inc(sy.dma_start(
                        out=xn2[:, a, :], in_=xnS[:, a, tau0:tau0 + 512]),
                        "dS", 16))
                for cc in range(4):
                    chunk = blk * 4 + cc
                    off = cc * 128
                    E.wait(te, "te", E.tok("vS"), E.tok("sS"))
                    for a in range(NI):
                        E.wait(te, "te", lds[a])
                        E.inc(te.matmul(lgp[:], xn2[:, a, off:off + 128],
                                        wgate_s[:, a, :], start=(a == 0),
                                        stop=(a == NI - 1)), "tS")
                        hm = E.inc(te.matmul(hp[:], xn2[:, a, off:off + 128],
                                             aT_s[:, a, :], start=(a == 0),
                                             stop=(a == NI - 1)), "tS")
                    E.wait(ve, "ve", hm)
                    r1 = E.inc(ve.reduce_max(rt[:, 0, 0:1], lgp[:], axis=AX),
                               "vS")
                    E.wait(ve, "ve", r1)
                    eqt = vTS(rt[:, 1, :], lgp[:], rt[:, 0, 0:1], OP.is_equal)
                    E.wait(ve, "ve", eqt)
                    e2 = E.inc(ve.scalar_tensor_tensor(
                        rt[:, 2, :], rt[:, 1, :], -1e30, lgp[:], OP.mult,
                        OP.add), "vS")
                    E.wait(ve, "ve", e2)
                    r2 = E.inc(ve.reduce_max(rt[:, 0, 1:2], rt[:, 2, :],
                                             axis=AX), "vS")
                    ng = vTS(rt[:, 0, 2:3], rt[:, 0, 0:1], -1.0, OP.mult)
                    E.wait(sc, "sc", ng)
                    ex = E.inc(sc.activation(rt[:, 3, :], lgp[:], AF.Exp,
                                             bias=rt[:, 0, 2:3]), "sS")
                    E.wait(ve, "ve", r2)
                    sl2 = vTS(rt[:, 1, :], lgp[:], rt[:, 0, 1:2], OP.is_ge)
                    E.wait(ve, "ve", ex, sl2)
                    nm = vTT(rt[:, 3, :], rt[:, 3, :], rt[:, 1, :], OP.mult)
                    E.wait(ve, "ve", nm)
                    dn = E.inc(ve.reduce_sum(rt[:, 0, 3:4], rt[:, 3, :],
                                             axis=AX), "vS")
                    E.wait(ve, "ve", dn)
                    rc = E.inc(ve.reciprocal(rt[:, 0, 4:5], rt[:, 0, 3:4]),
                               "vS")
                    E.wait(ve, "ve", rc)
                    rw = vTS(route[:, chunk, :], rt[:, 3, :], rt[:, 0, 4:5],
                             OP.mult)
                    E.wait(ve, "ve", rw)
                    hw = vTT(hwb[:].rearrange("p (e r) -> p e r", e=8),
                             hp[:].rearrange("p (e r) -> p e r", e=8),
                             route[:, chunk, :, None].broadcast_to(
                                 (128, 8, 16)), OP.mult)
                    E.wait(te, "te", hw)
                    tp = E.inc(te.transpose(htp[:], hwb[:], ident[:]), "tS")
                    E.wait(ve, "ve", tp)
                    vCP(hwfm[:, chunk * 128:(chunk + 1) * 128], htp[:])
            if dbg:
                E.barrier(ENGS)
                E.inc(sy.dma_start(out=hwdbg[:, :], in_=hwfm[:]), "dS", 16)
                E.inc(sy.dma_start(out=rtdbg[:, :, :], in_=route[:]), "dS", 16)
            E.barrier(ENGS)

        def proj(tag, win, mixsrc, dstS, act=None):
            with (
                nc.sbuf_tensor(f"mixh{tag}", [128, NI, 1024], BF16) as mixh,
                nc.sbuf_tensor(f"wtile{tag}", [128, 2, CH], BF16) as wtile,
                nc.sbuf_tensor(f"evb{tag}", [128, 2, 512], BF16) as evb,
                nc.psum_tensor(f"pp{tag}", [128, 2, 512], F32) as pp,
            ):
                prev_ev = [None, None]
                prev_sp = [None, None]
                for h in range(2):
                    E.wait(sy, "sy", E.tok("tS"))
                    ml = []
                    for a in range(NI):
                        ml.append(E.inc(sy.dma_start(
                            out=mixh[:, a, :],
                            in_=mixsrc[:, a, h * 1024:(h + 1) * 1024]),
                            "dS", 16))
                    wl = [None] * NI
                    E.wait(gp, "gp", E.tok("tS"))
                    wl[0] = E.inc(gp.dma_start(out=wtile[:, 0, :], in_=win[0]),
                                  "dG", 16)
                    wl[1] = E.inc(gp.dma_start(out=wtile[:, 1, :], in_=win[1]),
                                  "dG", 16)
                    for i in range(NI):
                        bufi = i % 2
                        if i == 0:
                            E.barrier(ENGS)
                        E.wait(te, "te", wl[i], prev_ev[0], prev_ev[1])
                        for a in range(NI):
                            E.wait(te, "te", ml[a])
                            for ts2 in range(2):
                                E.inc(te.matmul(
                                    pp[:, ts2, :],
                                    wtile[:, bufi, a * 128:(a + 1) * 128],
                                    mixh[:, a, ts2 * 512:(ts2 + 1) * 512],
                                    start=(a == 0), stop=(a == NI - 1)), "tS")
                        mmt = E.tok("tS")
                        if i + 2 < NI:
                            E.wait(gp, "gp", mmt)
                            wl[i + 2] = E.inc(gp.dma_start(
                                out=wtile[:, bufi, :], in_=win[i + 2]), "dG", 16)
                        for ts2 in range(2):
                            tau0 = h * 1024 + ts2 * 512
                            c0 = tau0 // 128
                            if act is None:
                                E.wait(ve, "ve", mmt, prev_sp[ts2])
                                ev = vCP(evb[:, ts2, :], pp[:, ts2, :])
                                eng, en, qs = sc, "sc", "dC"
                            else:
                                E.wait(sc, "sc", mmt, prev_sp[ts2])
                                ev = E.inc(sc.activation(
                                    evb[:, ts2, :], pp[:, ts2, :], act), "sS")
                                eng, en, qs = sy, "sy", "dS"
                            prev_ev[ts2] = ev
                            E.wait(eng, en, ev)
                            prev_sp[ts2] = E.inc(eng.dma_start(
                                out=dstS[:, c0:c0 + 4, i, :],
                                in_=evb[:, ts2, :].rearrange(
                                    "p (c t) -> p c t", c=4)), qs, 16)
            E.barrier(ENGS)

        proj("k", wins["wk"], mS[0], kS)
        if dbg:
            E.inc(sy.dma_start(out=kdbg[:, :, :, :], in_=kS[:, :, :, :]),
                  "dS", 16)
            E.barrier(ENGS)
        proj("v", wins["wv"], mS[1], vS)

        with (
            nc.sbuf_tensor("kfa", [128, NI, 128], BF16) as kfa,
            nc.sbuf_tensor("vfa", [128, NI, 128], BF16) as vfa,
            nc.sbuf_tensor("khf", [128, NI, 128], BF16) as khf,
            nc.sbuf_tensor("ktm", [128, NI, 128], BF16) as ktm,
            nc.sbuf_tensor("vtm", [128, NI, 128], BF16) as vtm,
            nc.psum_tensor("tpp", [128, 2, 128], BF16) as tpp,
            nc.psum_tensor("sdp", [128, NI, 64], F32) as sdp,
        ):
            prev_state = E.tok("vS")
            for c in range(NCHK):
                E.barrier(ENGS)
                E.wait(sy, "sy", E.tok("tS"))
                lk = E.inc(sy.dma_start(out=kfa[:], in_=kS[:, c, :, :]),
                           "dS", 16)
                lv = E.inc(sy.dma_start(out=vfa[:], in_=vS[:, c, :, :]),
                           "dS", 16)
                E.wait(ve, "ve", lk, E.tok("tS"))
                kh = vTT(khf[:].rearrange("p i (a t) -> p i a t", t=SUB),
                         kfa[:].rearrange("p i (a t) -> p i a t", t=SUB),
                         Dkf[:, :, None, :].broadcast_to(
                             (128, NI, 128 // SUB, SUB)), OP.mult)
                evt = [None, None]
                for i in range(NI):
                    par = i % 2
                    E.wait(te, "te", kh, lv, evt[par])
                    tk = E.inc(te.transpose(tpp[:, par, :], khf[:, i, :],
                                            ident[:]), "tS")
                    E.wait(ve, "ve", tk)
                    e1 = vCP(ktm[:, i, :], tpp[:, par, :])
                    E.wait(te, "te", e1)
                    tv = E.inc(te.transpose(tpp[:, par, :], vfa[:, i, :],
                                            ident[:]), "tS")
                    E.wait(ve, "ve", tv)
                    evt[par] = vCP(vtm[:, i, :], tpp[:, par, :])
                tdone = (evt[0], evt[1])
                for m in range(4):
                    M = c * 4 + m
                    E.wait(gp, "gp", prev_state)
                    spl = E.inc(gp.dma_start(
                        out=stS[:, M, :],
                        in_=Sst[:].rearrange("p i e -> p (i e)")), "dG", 16)
                    E.wait(te, "te", tdone[0], tdone[1], prev_state)
                    for i in range(NI):
                        for hh in range(2):
                            E.inc(te.matmul(
                                sdp[hh * 64:(hh + 1) * 64, i, :],
                                ktm[m * 32:(m + 1) * 32, i,
                                    hh * 64:(hh + 1) * 64],
                                vtm[m * 32:(m + 1) * 32, i,
                                    hh * 64:(hh + 1) * 64],
                                start=True, stop=True,
                                tile_position=(m * 32, hh * 64)), "tS")
                    um = E.tok("tS")
                    E.wait(ve, "ve", um, spl)
                    vTT(Sst[:], Sst[:],
                        wt32[:, :, None].broadcast_to((128, NI, 64)), OP.mult)
                    prev_state = vTT(Sst[:], Sst[:], sdp[:], OP.add)
            E.wait(gp, "gp", prev_state)
            stg = E.inc(gp.dma_start(
                out=ccin[:, :], in_=Sst[:].rearrange("p i e -> p (i e)")),
                "dG", 16)
            E.wait(gp, "gp", stg)
            E.inc(gp.collective_compute(
                "AllGather", OP.bypass,
                replica_groups=[[0, 1, 2, 3], [4, 5, 6, 7]],
                ins=[ccin.ap().opt()], outs=[ccout.ap().opt()]), "cS")
            E.barrier(ENGS)

        proj("r", wins["wr"], mS[2], rS)
        proj("g", wins["wg"], mS[3], gS, act=AF.Silu)

        with nc.sbuf_tensor("agb", [128, NI, 64], BF16) as agb:
            z0 = E.inc(ve.memset(Sin[:], 0.0), "vS")
            E.wait(sy, "sy", E.tok("cS"))
            for j in range(4):
                lj = E.inc(sy.dma_start(
                    out=agb[:].rearrange("p i e -> p (i e)"),
                    in_=ccout[j, :, :]), "dS", 16)
                E.wait(ve, "ve", lj, z0)
                vTT(agb[:], agb[:],
                    cinc[:, :, j:j + 1].broadcast_to((128, NI, 64)), OP.mult)
                z0 = vTT(Sin[:], Sin[:], agb[:], OP.add)
                E.wait(sy, "sy", z0)
            if dbg:
                E.inc(sy.dma_start(out=ccdbg[:, :, :], in_=ccout[:, :, :]),
                      "dS", 16)
            E.barrier(ENGS)

        with (
            nc.sbuf_tensor("kfb", [128, NI, 128], BF16) as kfb,
            nc.sbuf_tensor("rfb", [128, NI, 128], BF16) as rfb,
            nc.sbuf_tensor("vpd", [128, NI, 160], BF16) as vpd,
            nc.sbuf_tensor("Bb", [128, NI, 160], BF16) as Bb,
            nc.sbuf_tensor("sbb", [128, NI, 128], BF16) as sbb,
            nc.sbuf_tensor("rsc", [128, NI, 128], BF16) as rsc,
            nc.sbuf_tensor("stb", [128, 4, NI, 64], BF16) as stb,
            nc.sbuf_tensor("seftmp", [128, NI, 64], BF16) as seftmp,
            nc.sbuf_tensor("sef", [128, 4, NI, 64], BF16) as sef,
            nc.sbuf_tensor("Osb", [128, NI, 128], F32) as Osb,
            nc.sbuf_tensor("appt", [128, 8, 128], F32) as appt,
            nc.psum_tensor("abp", [128, 2, 8, 128], F32) as abp,
            nc.psum_tensor("oip", [128, 2, 8, 128], F32) as oip,
        ):
            E.inc(ve.memset(vpd[:, :, 0:32], 0.0), "vS")
            pz2 = E.inc(ve.memset(Bb[:, :, 0:32], 0.0), "vS")
            prev_osp = None
            for c in range(NCHK):
                E.barrier(ENGS)
                E.wait(sy, "sy", E.tok("tS"), E.tok("vS"))
                lk = E.inc(sy.dma_start(out=kfb[:], in_=kS[:, c, :, :]),
                           "dS", 16)
                lr = E.inc(sy.dma_start(out=rfb[:], in_=rS[:, c, :, :]),
                           "dS", 16)
                lv = E.inc(sy.dma_start(out=vpd[:, :, 32:160],
                                        in_=vS[:, c, :, :]), "dS", 16)
                lst = E.inc(sy.dma_start(
                    out=stb[:].rearrange("p m i e -> p (m i e)"),
                    in_=stS[:, 4 * c:4 * c + 4, :].rearrange(
                        "p m f -> p (m f)")), "dS", 16)
                seft = []
                E.wait(ve, "ve", lst, E.tok("tS"))
                for m in range(4):
                    M = 4 * c + m
                    vTT(seftmp[:], Sin[:],
                        wpow[:, :, M:M + 1].broadcast_to((128, NI, 64)),
                        OP.mult)
                    seft.append(vTT(sef[:, m, :, :], seftmp[:], stb[:, m, :, :],
                                    OP.add))
                E.wait(ve, "ve", lr)
                rsct = vTT(rsc[:].rearrange("p i (a t) -> p i a t", t=SUB),
                           rfb[:].rearrange("p i (a t) -> p i a t", t=SUB),
                           Dq[:, :, None, :].broadcast_to(
                               (128, NI, 128 // SUB, SUB)), OP.mult)
                E.wait(ve, "ve", lk, lv)
                vTT(Bb[:, :, 32:160], kfb[:],
                    cols[:, :, 1][:, :, None].broadcast_to((128, NI, 128)),
                    OP.mult)
                st0 = vTT(sbb[:], Bb[:, :, 32:160], rfb[:], OP.mult)
                sprev = st0
                binit_done = False
                o_init = [False] * 4
                prev_app = [E.tok("vS"), E.tok("vS")]
                for dl in range(BAND):
                    if dl >= 1:
                        E.wait(ve, "ve", E.tok("tS"))
                        if dl == 1:
                            E.wait(ve, "ve", pz2)
                            vCP(Bb[:, :, 32:160], kfb[:])
                        else:
                            vTT(Bb[:, :, 32:160], Bb[:, :, 32:160],
                                wcol[:, :, None].broadcast_to((128, NI, 128)),
                                OP.mult)
                        E.inc(ve.memset(
                            Bb[:, :, 32:160].rearrange(
                                "p i (b t) -> p i b t", t=SUB)
                            [:, :, :, SUB - dl:SUB - dl + 1], 0.0), "vS")
                        sprev = vTT(sbb[:], rfb[:], Bb[:, :, 32 - dl:160 - dl],
                                    OP.mult)
                    for g in range(4):
                        par = g % 2
                        E.wait(te, "te", sprev, prev_app[par])
                        rm = None
                        for ii in range(8):
                            i = g * 8 + ii
                            rm = E.inc(te.matmul(
                                abp[:, par, ii, :], indbd[:], sbb[:, i, :],
                                start=True, stop=True), "tS")
                        E.wait(ve, "ve", rm, prev_osp)
                        vTT(appt[:], abp[:, par, :, :],
                            vpd[:, g * 8:g * 8 + 8, 32 - dl:160 - dl], OP.mult)
                        if not o_init[g]:
                            prev_app[par] = vCP(Osb[:, g * 8:g * 8 + 8, :],
                                                appt[:])
                            o_init[g] = True
                        else:
                            prev_app[par] = vTT(Osb[:, g * 8:g * 8 + 8, :],
                                                Osb[:, g * 8:g * 8 + 8, :],
                                                appt[:], OP.add)
                for g in range(4):
                    par = g % 2
                    E.wait(te, "te", rsct, seft[3], prev_app[par])
                    ot = None
                    for m in range(4):
                        for ii in range(8):
                            i = g * 8 + ii
                            for hh in range(2):
                                ot = E.inc(te.matmul(
                                    oip[hh * 64:(hh + 1) * 64, par, ii,
                                        m * 32:(m + 1) * 32],
                                    sef[hh * 64:(hh + 1) * 64, m, i, :],
                                    rsc[hh * 64:(hh + 1) * 64, i,
                                        m * 32:(m + 1) * 32],
                                    start=True, stop=True,
                                    tile_position=(hh * 64, hh * 64)), "tS")
                    E.wait(ve, "ve", ot)
                    prev_app[par] = vTT(Osb[:, g * 8:g * 8 + 8, :],
                                        Osb[:, g * 8:g * 8 + 8, :],
                                        oip[:, par, :, :], OP.add)
                E.wait(sc, "sc", prev_app[0], prev_app[1])
                prev_osp = E.inc(sc.dma_start(
                    out=OSs[:, c, :, :],
                    in_=Osb[:].rearrange("p i t -> p (i t)").rearrange(
                        "p (i t) -> p i t", t=128)), "dC", 16)
            E.barrier(ENGS)

        with (
            nc.sbuf_tensor("Og", [128, NI, 128], F32) as Og,
            nc.sbuf_tensor("Obf", [128, NI, 128], BF16) as Obf,
            nc.sbuf_tensor("Osq", [128, NI, 128], BF16) as Osq,
            nc.sbuf_tensor("gfb", [128, NI, 128], BF16) as gfb,
            nc.sbuf_tensor("mnb", [128, NI, 128], F32) as mnb,
            nc.sbuf_tensor("msb", [128, NI, 128], F32) as msb,
            nc.sbuf_tensor("gnt", [128, NI, 128], F32) as gnt,
            nc.psum_tensor("gsp", [128, 2, 4, 128], F32) as gsp,
        ):
            for c in range(NCHK):
                E.barrier(ENGS)
                E.wait(sy, "sy", E.tok("vS"), E.tok("tS"), E.tok("dC"))
                lo = E.inc(sy.dma_start(
                    out=Og[:].rearrange("p i t -> p (i t)"),
                    in_=OSs[:, c, :, :].rearrange("p i t -> p (i t)")),
                    "dS", 16)
                lg = E.inc(sy.dma_start(out=gfb[:], in_=gS[:, c, :, :]),
                           "dS", 16)
                E.wait(ve, "ve", lo)
                ob = vCP(Obf[:], Og[:])
                E.wait(sc, "sc", ob)
                osq = E.inc(sc.activation(Osq[:], Obf[:], AF.Square), "sS")
                gst = [None, None]
                for gg in range(16):
                    par = gg % 2
                    E.wait(te, "te", ob, osq, gst[par])
                    for ii in range(2):
                        i = gg * 2 + ii
                        E.inc(te.matmul(gsp[:, par, 2 * ii, :], indgn[:],
                                        Obf[:, i, :], start=True, stop=True),
                              "tS")
                        gm = E.inc(te.matmul(gsp[:, par, 2 * ii + 1, :],
                                             indgn[:], Osq[:, i, :],
                                             start=True, stop=True), "tS")
                    E.wait(ve, "ve", gm)
                    vCP(mnb[:, gg * 2:gg * 2 + 2, :], gsp[:, par, 0::2, :])
                    gst[par] = vCP(msb[:, gg * 2:gg * 2 + 2, :],
                                   gsp[:, par, 1::2, :])
                vTT(gnt[:], mnb[:], mnb[:], OP.mult)
                vTT(gnt[:], msb[:], gnt[:], OP.subtract)
                c1 = vTS(gnt[:], gnt[:], EPS, OP.add)
                E.wait(sc, "sc", c1)
                E.inc(sc.activation(gnt[:], gnt[:], AF.Ln), "sS")
                c3 = E.inc(sc.activation(gnt[:], gnt[:], AF.Exp, scale=-0.5),
                           "sS")
                E.wait(ve, "ve", c3, lg)
                vTT(Og[:], Og[:], mnb[:], OP.subtract)
                vTT(Og[:], Og[:], gnt[:], OP.mult)
                vTT(Og[:], Og[:],
                    cols[:, :, 6][:, :, None].broadcast_to((128, NI, 128)),
                    OP.mult)
                vTT(Og[:], Og[:],
                    cols[:, :, 7][:, :, None].broadcast_to((128, NI, 128)),
                    OP.add)
                gngt = vTT(Obf[:], Og[:], gfb[:], OP.mult)
                E.wait(sc, "sc", gngt)
                E.inc(sc.dma_start(out=gngS[:, c, :, :],
                                   in_=Obf[:].rearrange(
                                       "p i t -> p i t")), "dC", 16)
            E.barrier(ENGS)

        with (
            nc.sbuf_tensor("xnh", [128, NI, 1024], BF16) as xnh,
            nc.sbuf_tensor("wtile6", [128, 2, CH], BF16) as wtile,
            nc.sbuf_tensor("evf", [128, 2, 512], F32) as evf,
            nc.psum_tensor("pp6", [128, 2, 512], F32) as pp6,
        ):
            prev_ev = [None, None]
            prev_sp = [None, None]
            for h in range(2):
                E.wait(sy, "sy", E.tok("tS"))
                ml = []
                for a in range(NI):
                    ml.append(E.inc(sy.dma_start(
                        out=xnh[:, a, :],
                        in_=xnS[:, a, h * 1024:(h + 1) * 1024]), "dS", 16))
                wl = [None] * NI
                E.wait(gp, "gp", E.tok("tS"))
                wl[0] = E.inc(gp.dma_start(out=wtile[:, 0, :], in_=wins["wb"][0]),
                              "dG", 16)
                wl[1] = E.inc(gp.dma_start(out=wtile[:, 1, :], in_=wins["wb"][1]),
                              "dG", 16)
                for oc in range(NI):
                    bufi = oc % 2
                    if oc == 0:
                        E.barrier(ENGS)
                    E.wait(te, "te", wl[oc], prev_ev[0], prev_ev[1])
                    for a in range(NI):
                        E.wait(te, "te", ml[a])
                        for ts2 in range(2):
                            E.inc(te.matmul(
                                pp6[:, ts2, :],
                                wtile[:, bufi, a * 128:(a + 1) * 128],
                                xnh[:, a, ts2 * 512:(ts2 + 1) * 512],
                                start=(a == 0), stop=False), "tS")
                    for ts2 in range(2):
                        tau0 = h * 1024 + ts2 * 512
                        E.inc(te.matmul(
                            pp6[:, ts2, :], bb_s[0:1, oc * 128:(oc + 1) * 128],
                            onesr[0:1, :], start=False, stop=False), "tS")
                        E.inc(te.matmul(
                            pp6[:, ts2, :], bflat_s[:, oc * 128:(oc + 1) * 128],
                            hwfm[:, tau0:tau0 + 512], start=False, stop=True),
                            "tS")
                    mmt = E.tok("tS")
                    if oc + 2 < NI:
                        E.wait(gp, "gp", mmt)
                        wl[oc + 2] = E.inc(gp.dma_start(
                            out=wtile[:, bufi, :], in_=wins["wb"][oc + 2]),
                            "dG", 16)
                    for ts2 in range(2):
                        tau0 = h * 1024 + ts2 * 512
                        E.wait(ve, "ve", mmt, prev_sp[ts2])
                        ev = vCP(evf[:, ts2, :], pp6[:, ts2, :])
                        prev_ev[ts2] = ev
                        E.wait(sc, "sc", ev)
                        prev_sp[ts2] = E.inc(sc.dma_start(
                            out=outP[:, oc, tau0:tau0 + 512],
                            in_=evf[:, ts2, :]), "dC", 16)
            E.barrier(ENGS)

        with (
            nc.sbuf_tensor("gnh", [128, NI, 1024], BF16) as gnh,
            nc.sbuf_tensor("wtile7", [128, 2, CH], BF16) as wtile,
            nc.sbuf_tensor("pstr", [128, 2, 512], F32) as pstr,
            nc.sbuf_tensor("xstr", [128, 2, 512], F32) as xstr,
            nc.sbuf_tensor("ofin", [128, 2, 512], F32) as ofin,
            nc.psum_tensor("pp7", [128, 2, 512], F32) as pp7,
        ):
            prev_ev = [None, None]
            prev_sp = [None, None]
            for h in range(2):
                E.wait(sy, "sy", E.tok("tS"))
                lgn = E.inc(sy.dma_start(
                    out=gnh[:].rearrange("p i t -> p (i t)"),
                    in_=gngS[:, 8 * h:8 * h + 8, :, :].rearrange(
                        "p c i t -> p (c i t)")), "dS", 16)
                wl = [None] * NI
                E.wait(gp, "gp", E.tok("tS"))
                wl[0] = E.inc(gp.dma_start(out=wtile[:, 0, :], in_=wins["wo"][0]),
                              "dG", 16)
                wl[1] = E.inc(gp.dma_start(out=wtile[:, 1, :], in_=wins["wo"][1]),
                              "dG", 16)
                gv = gnh[:].rearrange("p i t -> p (i t)").rearrange(
                    "p (c i t) -> p c i t", c=8, i=NI)
                for oc in range(NI):
                    bufi = oc % 2
                    if oc == 0:
                        E.barrier(ENGS)
                    E.wait(te, "te", wl[oc], lgn, prev_ev[0], prev_ev[1])
                    for a in range(NI):
                        for ts2 in range(2):
                            E.inc(te.matmul(
                                pp7[:, ts2, :],
                                wtile[:, bufi, a * 128:(a + 1) * 128],
                                gv[:, 4 * ts2:4 * ts2 + 4, a, :],
                                start=(a == 0), stop=(a == NI - 1)), "tS")
                    mmt = E.tok("tS")
                    if oc + 2 < NI:
                        E.wait(gp, "gp", mmt)
                        wl[oc + 2] = E.inc(gp.dma_start(
                            out=wtile[:, bufi, :], in_=wins["wo"][oc + 2]),
                            "dG", 16)
                    for ts2 in range(2):
                        tau0 = h * 1024 + ts2 * 512
                        E.wait(sy, "sy", prev_sp[ts2])
                        lp = E.inc(sy.dma_start(
                            out=pstr[:, ts2, :],
                            in_=outP[:, oc, tau0:tau0 + 512]), "dS", 16)
                        lx = E.inc(sy.dma_start(
                            out=xstr[:, ts2, :],
                            in_=xT[:, oc, tau0 + 1:tau0 + 513]), "dS", 16)
                        E.wait(ve, "ve", mmt, lp, lx, prev_sp[ts2])
                        vTT(ofin[:, ts2, :], pp7[:, ts2, :], pstr[:, ts2, :], OP.add)
                        ev = vTT(ofin[:, ts2, :], ofin[:, ts2, :], xstr[:, ts2, :],
                                 OP.add)
                        prev_ev[ts2] = ev
                        E.wait(sc, "sc", ev)
                        prev_sp[ts2] = E.inc(sc.dma_start(
                            out=outF[:, oc, tau0:tau0 + 512],
                            in_=ofin[:, ts2, :]), "dC", 16)
            E.barrier(ENGS)
    return nc


def _host_prep(inputs):
    bf = ml_dtypes.bfloat16
    x = np.asarray(inputs["L0_attn_hidden"], np.float32)

    def tile_w(w):
        return np.ascontiguousarray(
            np.asarray(w, np.float32).T.reshape(NI, 128, NI, 128)
            .transpose(2, 1, 0, 3).reshape(NI, 128, CH))

    wk = tile_w(inputs["L0_attn_w_key"])
    wv = tile_w(inputs["L0_attn_w_value"])
    wr = tile_w(inputs["L0_attn_w_receptance"])
    wg = tile_w(inputs["L0_attn_w_gate"])
    wo = tile_w(inputs["L0_attn_w_output"])
    wb = tile_w(inputs["L0_ffn_W_base"])
    wgate = np.ascontiguousarray(
        np.asarray(inputs["L0_ffn_W_gate"], np.float32).T
        .reshape(NI, 128, 8).transpose(1, 0, 2))
    A2 = np.asarray(inputs["L0_ffn_A"], np.float32).reshape(128, CH)
    aT = np.ascontiguousarray(A2.T.reshape(NI, 128, 128).transpose(1, 0, 2))
    bflat = np.ascontiguousarray(
        np.asarray(inputs["L0_ffn_B"], np.float32).transpose(0, 2, 1)
        .reshape(128, CH))

    def col(v):
        return np.asarray(v, np.float32).reshape(NI, 128).T

    cols = np.stack([
        col(inputs["L0_attn_time_decay"].reshape(CH)),
        col(inputs["L0_attn_time_faaaa"].reshape(CH)),
        col(inputs["L0_attn_time_mix_key"]),
        col(inputs["L0_attn_time_mix_value"]),
        col(inputs["L0_attn_time_mix_receptance"]),
        col(inputs["L0_attn_time_mix_gate"]),
        col(inputs["L0_attn_gn_weight"]),
        col(inputs["L0_attn_gn_bias"]),
    ], axis=2)
    cols = np.ascontiguousarray(cols)
    bb = np.asarray(inputs["L0_ffn_b_base"], np.float32).reshape(1, CH)

    ident = np.eye(128, dtype=bf)
    onesc = np.full((128, 128), 1.0 / CH, dtype=bf)
    blk = np.zeros((128, 128), np.float32)
    blk[0:64, 0:64] = 1.0
    blk[64:128, 64:128] = 1.0
    indbd = blk.astype(bf)
    indgn = (blk / 64.0).astype(bf)
    onesr = np.ones((1, 512), dtype=bf)
    tmod = np.arange(SUB, dtype=np.float32) % SUB
    EQ = np.broadcast_to(tmod, (128, SUB)).copy()
    EK = np.broadcast_to(SUB - 1 - tmod, (128, SUB)).astype(np.float32).copy()
    EM = np.broadcast_to(32.0 * np.arange(64, dtype=np.float32),
                         (128, 64)).copy()

    shared = dict(wk=wk, wv=wv, wr=wr, wg=wg, wo=wo, wb=wb, wgate=wgate,
                  aT=aT, bflat=bflat, cols=cols, bb=bb, identI=ident,
                  onescI=onesc, indgnI=indgn, indbdI=indbd, onesrI=onesr,
                  EQI=EQ, EKI=EK, EMI=EM)

    in_maps = []
    for core in range(8):
        b = core // 4
        q = core % 4
        seg = np.zeros((TOK + 1, CH), np.float32)
        seg[1:] = x[b, q * TOK:(q + 1) * TOK]
        if q > 0:
            seg[0] = x[b, q * TOK - 1]
        xTc = np.ascontiguousarray(
            seg.T.reshape(NI, 128, TOK + 1).transpose(1, 0, 2))
        ce = np.zeros(4, np.float32)
        for j in range(4):
            ce[j] = 2048.0 * (q - 1 - j) if j < q else 1e9
        cexp = np.broadcast_to(ce, (128, 4)).copy()
        m = dict(shared)
        m["xT"] = xTc
        m["cexpI"] = cexp
        in_maps.append(m)
    return in_maps


def kernel(**inputs):
    in_maps = _host_prep(inputs)
    nc = build_nc()
    res = run_bass_kernel_spmd(nc, in_maps, core_ids=list(range(8)))
    out = np.empty((B, S, CH), np.float32)
    for core in range(8):
        b, q = core // 4, core % 4
        of = np.asarray(res.results[core]["outF"])
        out[b, q * TOK:(q + 1) * TOK] = (
            of.transpose(1, 0, 2).reshape(CH, TOK).T)
    return out
